# version 06
# baseline (speedup 1.0000x reference)
import numpy as np
import ml_dtypes
import concourse.bass as bass
import concourse.mybir as mybir
from concourse.bass_utils import run_bass_kernel_spmd

F32 = mybir.dt.float32
BF16 = mybir.dt.bfloat16
AF = mybir.ActivationFunctionType
ALU = mybir.AluOpType
AX = mybir.AxisListType

BF = ml_dtypes.bfloat16

B, T, C = 16, 2048, 1024
H, D = 16, 64
NCORES = 8
BPC = B // NCORES
E, I = 16, 1024
NHG, HPG = 4, 4
DC = C // 128
TBS = 512
NTB_B = T // TBS
NTB = BPC * NTB_B
NTT = (BPC * T) // 128
ROUTE_SCALE = 2.5
LN_EPS = 1e-5
EXP_SHIFT = -8.0
NEG = -1.0e30

_SLOPE0 = 2.0 ** (-0.5)
SLOPES = [_SLOPE0 * _SLOPE0 ** i for i in range(H)]

WINDOW_LOGCUT = 100.0
ENABLE_WINDOW = False

_TAPS = ()


ENGS = ("tensor", "vector", "scalar", "gpsimd", "sync")


def _rn(x):
    if isinstance(x, str):
        return x
    t = getattr(x, "tensor", None)
    return t.name if t is not None else x.name


SELF_SYNC = ("vector", "scalar", "gpsimd")


class P:
    def __init__(self):
        self.items = {e: [] for e in ENGS}
        self.cnt = {e: 0 for e in ENGS}
        self.dma_cnt = {}
        self.vc = {e: {e: 0} for e in ENGS}
        self.hist = {e: {0: {e: 0}} for e in ENGS}
        self.res = {}
        self.self_waited = {e: 0 for e in ENGS}
        self.n_waits = 0

    def _deps(self, reads, writes):
        d = []
        for r in reads:
            ent = self.res.get(r)
            if ent and ent[0]:
                d.append(ent[0])
        for w in writes:
            ent = self.res.get(w)
            if ent:
                if ent[0]:
                    d.append(ent[0])
                d.extend(ent[1].items())
        return d

    def _upd(self, src, cnt, reads, writes):
        for r in reads:
            self.res.setdefault(r, [None, {}])[1][src] = cnt
        for w in writes:
            self.res[w] = [(src, cnt), {}]

    def _resolve(self, eng, deps):
        waits = []
        vc = self.vc[eng]
        for src, cnt in sorted(set(deps)):
            if src == eng:
                if eng in SELF_SYNC and cnt > self.self_waited[eng]:
                    waits.append((eng, cnt))
                    self.self_waited[eng] = cnt
                continue
            if vc.get(src, 0) >= cnt:
                continue
            waits.append((src, cnt))
            for s2, c2 in self.hist[src][cnt].items():
                if vc.get(s2, 0) < c2:
                    vc[s2] = c2
        self.n_waits += len(waits)
        return waits

    def _adv(self, src):
        self.cnt[src] = self.cnt.get(src, 0) + 1
        c = self.cnt[src]
        self.vc[src][src] = c
        self.hist[src][c] = dict(self.vc[src])
        return c

    def op(self, eng, fn, reads=(), writes=(), lp=False):
        reads = [_rn(x) for x in reads]
        writes = [_rn(x) for x in writes]
        waits = self._resolve(eng, self._deps(reads, writes))
        c = self._adv(eng)
        self._upd(eng, c, reads, writes)
        self.items[eng].append(("op", waits, fn, lp))
        return (eng, c)

    def dma(self, eng, stream, out, in_, reads_extra=(), **kw):
        r_in = [_rn(in_)] + [_rn(x) for x in reads_extra]
        w_out = [_rn(out)]
        ps = "dma:" + stream
        if ps not in self.vc:
            self.vc[ps] = {ps: 0}
            self.hist[ps] = {0: {ps: 0}}
            self.cnt[ps] = 0
            self.dma_cnt[stream] = 0
        waits = self._resolve(eng, self._deps(r_in, w_out))
        for s, c in self.vc[eng].items():
            if self.vc[ps].get(s, 0) < c:
                self.vc[ps][s] = c
        k = self._adv(ps)
        self.dma_cnt[stream] = k
        self._upd(ps, k, r_in, w_out)
        self.items[eng].append(("dma", waits, stream, out, in_, kw))
        return (ps, k)

    def wait_all(self, eng):
        deps = [(s, c) for s, c in self.cnt.items() if s != eng and c > 0]
        waits = self._resolve(eng, deps)
        if waits:
            self.items[eng].append(("waitonly", waits))

    def fence(self):
        for e in ENGS:
            self.wait_all(e)

    def emit(self, nc):
        sems = {e: nc.alloc_semaphore("sem_" + e) for e in ENGS}
        for s in self.dma_cnt:
            sems["dma:" + s] = nc.alloc_semaphore("semd_" + s)

        def wv(src, cnt):
            return 16 * cnt if src.startswith("dma:") else cnt

        def mk(eng_name):
            def run(engine):
                for it in self.items[eng_name]:
                    kind = it[0]
                    for (src, cnt) in it[1]:
                        engine.wait_ge(sems[src], wv(src, cnt))
                    if kind == "op":
                        if it[3]:
                            with nc.allow_low_precision(reason="deliberate bf16"):
                                it[2](engine).then_inc(sems[eng_name], 1)
                        else:
                            it[2](engine).then_inc(sems[eng_name], 1)
                    elif kind == "dma":
                        engine.dma_start(out=it[3], in_=it[4], **it[5]).then_inc(
                            sems["dma:" + it[2]], 16
                        )
            return run

        with nc.Block() as block:
            for e in ENGS:
                if self.items[e]:
                    getattr(block, e)(mk(e))
        return sems



def _split3(v):
    t1 = v.astype(BF).astype(np.float32)
    r = v - t1
    t2 = r.astype(BF).astype(np.float32)
    t3 = (r - t2).astype(BF).astype(np.float32)
    return t1.astype(BF), t2.astype(BF), t3.astype(BF)


def _prep_shared(inp):
    W_attn = np.asarray(inp["L0_attn_W_attn"], np.float32)
    W_proj = np.asarray(inp["L0_attn_W_proj"], np.float32)
    b_attn = np.asarray(inp["L0_attn_b_attn"], np.float32)
    b_proj = np.asarray(inp["L0_attn_b_proj"], np.float32)
    temp = np.asarray(inp["L0_attn_temperature"], np.float32).reshape(1, 1)
    Wg = np.asarray(inp["L0_ffn_Wg"], np.float32)
    W1 = np.asarray(inp["L0_ffn_W1"], np.float32)
    b1 = np.asarray(inp["L0_ffn_b1"], np.float32)
    W2 = np.asarray(inp["L0_ffn_W2"], np.float32)
    b2 = np.asarray(inp["L0_ffn_b2"], np.float32)
    W3 = np.asarray(inp["L0_ffn_W3"], np.float32)
    b3 = np.asarray(inp["L0_ffn_b3"], np.float32)

    d = {}
    wqk = np.empty((NHG, 4, 128, DC * 128), np.float32)
    for hg in range(NHG):
        for m in range(4):
            col0 = (0 if m < 2 else C) + hg * 256 + (m % 2) * 128
            blk = W_attn[:, col0:col0 + 128]
            wqk[hg, m] = blk.reshape(DC, 128, 128).transpose(1, 0, 2).reshape(128, -1)
    d["wqk"] = wqk
    wv = np.empty((NHG, 128, DC * 256), np.float32)
    for hg in range(NHG):
        blk = W_attn[:, 2 * C + hg * 256: 2 * C + (hg + 1) * 256]
        wv[hg] = blk.reshape(DC, 128, 256).transpose(1, 0, 2).reshape(128, -1)
    d["wv"] = wv
    wp = np.empty((DC, 128, DC * 128), np.float32)
    for m in range(DC):
        blk = W_proj[:, m * 128:(m + 1) * 128]
        wp[m] = blk.reshape(DC, 128, 128).transpose(1, 0, 2).reshape(128, -1)
    d["wp"] = wp
    d["hw1"] = np.ascontiguousarray(
        W1.reshape(E, 8, 128, 8, 128).transpose(0, 4, 1, 3, 2).reshape(E, 128, 8192))
    d["hw3"] = np.ascontiguousarray(
        W3.reshape(E, 8, 128, 8, 128).transpose(0, 4, 1, 3, 2).reshape(E, 128, 8192))
    d["hw2"] = np.ascontiguousarray(
        W2.reshape(E, 8, 128, 8, 128).transpose(0, 4, 1, 3, 2).reshape(E, 128, 8192))
    d["wgt"] = np.ascontiguousarray(
        Wg.T.reshape(DC, 128, E).transpose(1, 0, 2).reshape(128, DC * E))
    d["batt"] = np.ascontiguousarray(b_attn.reshape(24, 128).T)
    d["bav"] = np.ascontiguousarray(b_attn[2 * C:].reshape(1, C))
    d["bp"] = np.ascontiguousarray(b_proj.reshape(8, 128).T)
    d["b1t"] = np.ascontiguousarray(
        b1.T.reshape(8, 128, E).transpose(1, 0, 2).reshape(128, 8 * E))
    d["b3t"] = np.ascontiguousarray(
        b3.T.reshape(8, 128, E).transpose(1, 0, 2).reshape(128, 8 * E))
    d["b2"] = np.ascontiguousarray(b2)
    d["temp"] = temp
    return d


TABW = 2432


def _consts():
    c = {}
    c["ident"] = np.eye(128, dtype=np.float32)
    ii = np.arange(128)[:, None]
    ww = np.arange(TABW)[None, :]
    rel = (ii - ww + 384).astype(np.float32)
    c["tab"] = np.where(ii > ww - 384, -1.0e33, rel).astype(np.float32)
    oh = np.zeros((E, E * 128), np.float32)
    for e in range(E):
        oh[e, e * 128:(e + 1) * 128] = 1.0
    c["oh"] = oh.astype(BF)
    c["ones1"] = np.ones((1, 128), np.float32)
    c["eps"] = np.full((1, 1), LN_EPS, np.float32)
    c["ones64"] = np.ones((128, 64), np.float32).astype(BF)
    c["imms"] = np.full((128, 1), EXP_SHIFT, np.float32)
    return c



def _kcs(h, qb):
    kmax = 4 * qb + 3
    kmin = 0
    if ENABLE_WINDOW:
        lim = qb * TBS - WINDOW_LOGCUT / SLOPES[h]
        kmin = max(0, int(np.ceil((lim - 127.0) / 128.0)))
        kmin = min(kmin, kmax)
    return list(range(kmin, kmax + 1))


def build(taps=()):
    nc = bass.Bass("TRN2", target_bir_lowering=False)

    d_x = nc.dram_tensor("xcm", [BPC, C, T], F32, kind="ExternalInput")
    d_wqk = nc.dram_tensor("wqk", [NHG, 4, 128, DC * 128], F32, kind="ExternalInput")
    d_wv = nc.dram_tensor("wv", [NHG, 128, DC * 256], F32, kind="ExternalInput")
    d_wp = nc.dram_tensor("wp", [DC, 128, DC * 128], F32, kind="ExternalInput")
    d_hw1 = nc.dram_tensor("hw1", [E, 128, 8192], F32, kind="ExternalInput")
    d_hw3 = nc.dram_tensor("hw3", [E, 128, 8192], F32, kind="ExternalInput")
    d_hw2 = nc.dram_tensor("hw2", [E, 128, 8192], F32, kind="ExternalInput")
    d_wgt = nc.dram_tensor("wgt", [128, DC * E], F32, kind="ExternalInput")
    d_batt = nc.dram_tensor("batt", [128, 24], F32, kind="ExternalInput")
    d_bav = nc.dram_tensor("bav", [1, C], F32, kind="ExternalInput")
    d_bp = nc.dram_tensor("bp", [128, 8], F32, kind="ExternalInput")
    d_b1t = nc.dram_tensor("b1t", [128, 8 * E], F32, kind="ExternalInput")
    d_b3t = nc.dram_tensor("b3t", [128, 8 * E], F32, kind="ExternalInput")
    d_b2 = nc.dram_tensor("b2", [E, I], F32, kind="ExternalInput")
    d_temp = nc.dram_tensor("temp", [1, 1], F32, kind="ExternalInput")
    d_out = nc.dram_tensor("out", [BPC, C, T], F32, kind="ExternalOutput")

    cns = _consts()
    d_ident = nc.inline_tensor(cns["ident"], "c_ident")
    d_tab = nc.inline_tensor(cns["tab"], "c_tab")
    d_oh = nc.inline_tensor(np.asarray(cns["oh"]), "c_oh")
    d_imms = nc.inline_tensor(cns["imms"], "c_imms")
    d_ones1 = nc.inline_tensor(cns["ones1"], "c_ones1")
    d_ones64 = nc.inline_tensor(np.asarray(cns["ones64"]), "c_ones64")
    d_eps = nc.inline_tensor(cns["eps"], "c_eps")

    tap_outs = {}
    for t_ in taps:
        shp = {"x2": [128, DC * BPC * T], "we": [128, NTT * E],
               "yacc": [128, DC * BPC * T], "yb0": [128, DC * T],
               "q0": [128, 2 * T], "k0": [128, 2 * T], "v0": [128, 16 * 256],
               "sall": [128, NTT * E]}[t_]
        tap_outs[t_] = nc.dram_tensor("tap_" + t_, shp, F32, kind="ExternalOutput")

    guard = nc.alloc_sbuf_tensor("ringguard", [128, 256], mybir.dt.uint8)
    gml = nc.lookup_mloc(guard)
    gaddr = gml.addr() if callable(gml.addr) else gml.addr
    base = (int(gaddr) + 256 + 256 + 31) & ~31
    KB = 1024

    def at(name, shape, dtype, off):
        t = nc.alloc_sbuf_tensor_at(name, shape, dtype, offset=base + off)
        return t

    x2 = at("x2", [128, DC * BPC * T], BF16, 0)
    o_xb = 64 * KB
    Xb = at("Xb", [128, DC * T], BF16, o_xb)
    y_lo = at("y_lo", [128, 4 * BPC * T], BF16, o_xb)
    o_yb = 96 * KB
    Yb = at("Yb", [128, DC * T], BF16, o_yb)
    y_hi = at("y_hi", [128, 4 * BPC * T], BF16, o_yb)
    o_q = 128 * KB
    Qs = at("Qs", [128, 2 * T], BF16, o_q)
    Ks = at("Ks", [128, 2 * T], BF16, o_q + 8 * KB)
    HP0 = at("HP0", [128, 8 * TBS], BF16, o_q)
    g_s = at("g_s", [128, NTT * E], F32, o_q)
    g_ms = at("g_ms", [128, NTT * E], F32, o_q + 2 * KB)
    g_ms0 = at("g_ms0", [128, NTT * E], F32, o_q + 4 * KB)
    g_we = at("g_we", [128, NTT * E], F32, o_q + 6 * KB)
    g_gs = at("g_gs", [128, NTT * 4], F32, o_q + 8 * KB)
    g_r1 = at("g_r1", [128, NTT], F32, o_q + 8 * KB + 512)
    g_r2 = at("g_r2", [128, NTT], F32, o_q + 8 * KB + 640)
    g_r3 = at("g_r3", [128, NTT], F32, o_q + 8 * KB + 768)
    g_r4 = at("g_r4", [128, NTT], F32, o_q + 8 * KB + 896)
    g_pen = at("g_pen", [128, NTT * 4], F32, o_q + 10 * KB)
    g_wu = at("g_wu", [128, NTT * E], F32, o_q + 11 * KB)
    o_v = 144 * KB
    Vs = at("Vs", [128, 16 * 260], BF16, o_v)
    WE = at("WE", [E, NTT * 128], BF16, o_v)
    o_ar = 152 * KB
    W1s = at("W1s", [128, 8192], BF16, o_ar)
    W3s = at("W3s", [128, 8192], BF16, o_ar + 16 * KB)
    W2s = at("W2s", [128, 8192], BF16, o_ar + 32 * KB)
    WRS0 = at("WRS0", [128, TBS], BF16, o_ar + 48 * KB)
    WRS1 = at("WRS1", [128, TBS], BF16, o_ar + 49 * KB)
    XO0 = at("XO0", [128, DC * TBS], F32, o_ar)
    XO1 = at("XO1", [128, DC * TBS], F32, o_ar + 16 * KB)
    a = o_ar + 512
    WQK0 = at("WQK0", [128, DC * 128], BF16, a); a += 2 * KB
    WQK1 = at("WQK1", [128, DC * 128], BF16, a); a += 2 * KB
    WVs = at("WVs", [128, DC * 256], BF16, a); a += 4 * KB
    PT0 = at("PT0", [128, TBS], BF16, a); a += KB
    PT1 = at("PT1", [128, TBS], BF16, a); a += KB
    PT2 = at("PT2", [128, TBS], BF16, a); a += KB
    LNT = at("LNT", [65, TBS], F32, a); a += 2 * KB
    LINV = at("LINV", [128, TBS], BF16, a); a += KB
    LRS = at("LRS", [64, TBS], BF16, a); a += KB
    BRV = at("BRV", [128, C], BF16, a); a += 2 * KB
    BAV = at("BAV", [1, C], F32, o_ar + 32 * KB)
    IMMS = at("IMMS", [128, 1], F32, a); a += 32
    WP0 = at("WP0", [128, DC * 128], BF16, a); a += 2 * KB
    WP1 = at("WP1", [128, DC * 128], BF16, a); a += 2 * KB
    TAB = at("TAB", [128, TABW], F32, a); a += 9728
    assert a - o_ar <= 32 * KB, a - o_ar
    XR = at("XR", [128, DC * TBS], BF16, o_ar + 32 * KB)
    MROW = at("MROW", [1, TBS], F32, o_ar + 40 * KB)
    SROW = at("SROW", [1, TBS], F32, o_ar + 42 * KB)
    ALPHA = at("ALPHA", [1, TBS], F32, o_ar + 44 * KB)
    BETA = at("BETA", [1, TBS], F32, o_ar + 46 * KB)

    s = 202 * KB
    IDE = at("IDE", [128, 128], F32, o_q + 13 * KB + 512)
    ONES1 = at("ONES1", [1, 128], F32, s); s += 512
    ONES64 = at("ONES64", [128, 64], BF16, s); s += 128
    ONESC = ONES64
    BATT = at("BATT", [128, 24], F32, s); s += 96
    BQS = at("BQS", [128, 8], F32, s); s += 32
    BP = at("BP", [128, 8], F32, s); s += 32
    INV8T = at("INV8T", [128, 1], F32, s); s += 32
    TEMP = at("TEMP", [1, 1], F32, s); s += 32
    TROW = at("TROW", [1, 1], F32, s); s += 32
    TROW2 = at("TROW2", [1, 1], F32, s); s += 32
    EPSC = at("EPSC", [1, 1], F32, s); s += 32
    WGT = at("WGT", [128, DC * E], BF16, s); s += 256
    B1T = at("B1T", [128, 8 * E], F32, s); s += 512
    B3T = at("B3T", [128, 8 * E], F32, s); s += 512
    B2B = at("B2B", [E, I], BF16, o_q + 14 * KB)
    OHS = at("OHS", [E, E * 128], BF16, o_q + 8 * KB)
    TN = at("TN", [128, TBS], BF16, s); s += KB
    XSQ0 = at("XSQ0", [128, TBS], BF16, s); s += KB
    XSQ1 = at("XSQ1", [128, TBS], BF16, s); s += KB
    assert base + s <= nc.sbuf_top, (base, s, nc.sbuf_top)

    pb = [nc.alloc_psum_tensor(f"pb{i}", [128, TBS], F32) for i in range(8)]

    p = P()

    p.dma("sync", "cst", IMMS[:, :], d_imms.ap())
    p.dma("sync", "cst", TAB[:, :], d_tab.ap())
    p.dma("gpsimd", "wgt", WGT[:, :], d_wgt.ap())
    p.dma("sync", "cst", BATT[:, :], d_batt.ap())
    p.dma("sync", "cst", BAV[:, :], d_bav.ap())
    p.dma("sync", "cst", BP[:, :], d_bp.ap())
    p.dma("sync", "cst", TEMP[:, :], d_temp.ap())
    p.dma("sync", "cst", ONES1[:, :], d_ones1.ap())
    p.dma("sync", "cst", ONES64[:, :], d_ones64.ap())
    p.dma("sync", "cst", EPSC[:, :], d_eps.ap())
    p.op("vector", lambda e: e.tensor_scalar(TROW[:, :], TEMP[:, :], 8.0, None,
                                             ALU.mult),
         reads=[TEMP], writes=[TROW])
    p.op("vector", lambda e: e.reciprocal(TROW2[:, :], TROW[:, :]),
         reads=[TROW], writes=[TROW2])
    p.op("tensor", lambda e: e.matmul(pb[0][:, 0:1], ONES1[:, :], TROW2[:, :],
                                      start=True, stop=True),
         reads=[ONES1, TROW2], writes=[pb[0]])
    p.op("scalar", lambda e: e.copy(INV8T[:, :], pb[0][:, 0:1]),
         reads=[pb[0]], writes=[INV8T])
    for i in range(8):
        p.op("vector", lambda e, i=i: e.tensor_tensor(
            BQS[:, i:i + 1], BATT[:, i:i + 1], INV8T[:, :], ALU.mult),
            reads=[BATT, INV8T], writes=[BQS])
    for half in range(2):
        p.op("tensor", lambda e, half=half: e.matmul(
            pb[1][:, :], ONES1[:, :], BAV[:, half * TBS:(half + 1) * TBS],
            start=True, stop=True),
            reads=[ONES1, BAV], writes=[pb[1]])
        p.op("scalar", lambda e, half=half: e.copy(
            BRV[:, half * TBS:(half + 1) * TBS], pb[1][:, :]),
            reads=[pb[1]], writes=[BRV])
    p.op("vector", lambda e: e.tensor_copy(
        Vs[:, :].rearrange("p (x e) -> p x e", e=65)[:, :, 64:65],
        ONES64[:, 0:64].unsqueeze(2)),
        reads=[ONES64], writes=[Vs])
    p.fence()

    wqk_slots = [WQK0, WQK1]

    def hg_prefetch(hg2):
        p.dma("gpsimd", "wgt", wqk_slots[0][:, :], d_wqk.ap()[hg2, 0])
        p.dma("gpsimd", "wgt", wqk_slots[1][:, :], d_wqk.ap()[hg2, 1])
        p.dma("gpsimd", "wgt", WVs[:, :], d_wv.ap()[hg2])

    hg_prefetch(0)
    for b in range(BPC):
        p.dma("gpsimd", "wgt", Xb[:, :].rearrange("p (c t) -> p c t", c=DC),
              d_x.ap()[b].rearrange("(c p) t -> p c t", p=128))
        pend = [None]

        for hg in range(NHG):
            for m in range(4):
                slot = wqk_slots[m % 2]
                dst = Qs if m < 2 else Ks
                fc = m if m < 2 else m - 2
                for tb in range(NTB_B):
                    bank = pb[tb % 2]
                    for cc in range(DC):
                        p.op("tensor", lambda e, bank=bank, slot=slot, cc=cc, tb=tb:
                             e.matmul(bank[:, :], slot[:, cc * 128:(cc + 1) * 128],
                                      Xb[:, cc * T + tb * TBS: cc * T + (tb + 1) * TBS],
                                      start=(cc == 0), stop=(cc == DC - 1)),
                             reads=[slot, Xb], writes=[bank])
                    if m < 2:
                        qi = 2 * hg + m
                        p.op("scalar", lambda e, bank=bank, dst=dst, fc=fc, tb=tb, qi=qi:
                             e.activation(dst[:, fc * T + tb * TBS: fc * T + (tb + 1) * TBS],
                                          bank[:, :], AF.Identity,
                                          bias=BQS[:, qi:qi + 1], scale=INV8T[:, 0:1]),
                             reads=[bank, BQS, INV8T], writes=[dst])
                    else:
                        ki = 8 + 2 * hg + (m - 2)
                        p.op("scalar", lambda e, bank=bank, dst=dst, fc=fc, tb=tb, ki=ki:
                             e.activation(dst[:, fc * T + tb * TBS: fc * T + (tb + 1) * TBS],
                                          bank[:, :], AF.Identity,
                                          bias=BATT[:, ki:ki + 1]),
                             reads=[bank, BATT], writes=[dst])
                if m + 2 < 4:
                    p.dma("gpsimd", "wgt", wqk_slots[(m + 2) % 2][:, :],
                          d_wqk.ap()[hg, m + 2])
            for tt in range(16):
                bank = pb[tt % 2]
                for cc in range(DC):
                    p.op("tensor", lambda e, bank=bank, cc=cc, tt=tt: e.matmul(
                        bank[:, 0:256],
                        Xb[:, cc * T + tt * 128: cc * T + (tt + 1) * 128],
                        WVs[:, cc * 256:(cc + 1) * 256],
                        start=(cc == 0), stop=(cc == DC - 1)),
                        reads=[Xb, WVs], writes=[bank])
                p.op("vector", lambda e, bank=bank, tt=tt, hg=hg: e.tensor_tensor(
                    Vs[:, tt * 260:tt * 260 + 260]
                    .rearrange("p (h v) -> p h v", v=65)[:, :, 0:64],
                    bank[:, 0:256].rearrange("p (h d) -> p h d", d=64),
                    BRV[:, hg * 256:(hg + 1) * 256]
                    .rearrange("p (h d) -> p h d", d=64), ALU.add),
                    reads=[bank, BRV], writes=[Vs])
            if b * NHG + hg + 1 < BPC * NHG:
                hg_prefetch((hg + 1) % NHG)

            for hh in range(HPG):
                h = hg * HPG + hh
                par = hh % 2
                fc = hh // 2
                rlo = par * 64
                slope = SLOPES[h]
                for qb in range(NTB_B):
                    kcs = _kcs(h, qb)
                    ybank = pb[5 + (qb % 2)]
                    pts = [PT0, PT1, PT2]
                    nk = len(kcs)

                    def s_tile(i):
                        kc = kcs[i]
                        sbank = pb[2 + (i % 3)]
                        p.op("tensor", lambda e, sbank=sbank, kc=kc, qb=qb, fc=fc, rlo=rlo:
                             e.matmul(sbank[:, :],
                                      Ks[rlo:rlo + 64, fc * T + kc * 128: fc * T + (kc + 1) * 128],
                                      Qs[rlo:rlo + 64, fc * T + qb * TBS: fc * T + (qb + 1) * TBS],
                                      start=True, stop=True),
                             reads=[Ks, Qs], writes=[sbank])
                        pt = pts[i % 3]
                        wo = 384 - 128 * (kc - 4 * qb)
                        p.op("vector", lambda e, sbank=sbank, wo=wo, sl=slope:
                             e.scalar_tensor_tensor(sbank[:, :],
                                                    TAB[:, wo:wo + TBS], sl,
                                                    sbank[:, :],
                                                    ALU.mult, ALU.add),
                             reads=[sbank, TAB], writes=[sbank])
                        p.op("scalar", lambda e, sbank=sbank, pt=pt:
                             e.activation(pt[:, :], sbank[:, :], AF.Exp,
                                          bias=IMMS[:, 0:1]),
                             reads=[sbank, IMMS], writes=[pt])

                    def pv_tile(i):
                        kc = kcs[i]
                        pt = pts[i % 3]
                        st, sp = (i == 0), (i == nk - 1)
                        p.op("tensor", lambda e, pt=pt, kc=kc, hh=hh, st=st, sp=sp, ybank=ybank:
                             e.matmul(ybank[0:65, :],
                                      Vs[:, kc * 260 + hh * 65: kc * 260 + hh * 65 + 65],
                                      pt[:, :], start=st, stop=sp),
                             reads=[Vs, pt], writes=[ybank])

                    for i in range(nk):
                        s_tile(i)
                        if i == 1 and pend[0] is not None:
                            pend[0]()
                            pend[0] = None
                        if i >= 1:
                            pv_tile(i - 1)
                    pv_tile(nk - 1)

                    def mk_tail(ybank=ybank, h=h, qb=qb):
                        def tail():
                            p.op("scalar", lambda e: e.activation(
                                LNT[64:65, :], ybank[64:65, :], AF.Ln),
                                reads=[ybank], writes=[LNT])
                            p.op("scalar", lambda e: e.activation(
                                LINV[64:65, :], LNT[64:65, :], AF.Exp,
                                scale=-1.0),
                                reads=[LNT], writes=[LINV], lp=True)
                            p.op("tensor", lambda e: e.matmul(
                                pb[7][0:64, :], ONES64[64:65, :],
                                LINV[64:65, :],
                                start=True, stop=True, tile_position=(64, 0)),
                                reads=[ONES64, LINV], writes=[pb[7]])
                            p.op("scalar", lambda e: e.copy(
                                LRS[:, :], pb[7][0:64, :]),
                                reads=[pb[7]], writes=[LRS])
                            yoff = (h // 2) * T + qb * TBS
                            ypar = (h % 2) * 64
                            p.op("vector", lambda e: e.tensor_tensor(
                                Yb[ypar:ypar + 64, yoff:yoff + TBS],
                                ybank[0:64, :], LRS[:, :], ALU.mult),
                                reads=[ybank, LRS], writes=[Yb])
                        return tail
                    pend[0] = mk_tail()

        if pend[0] is not None:
            pend[0]()
            pend[0] = None

        wp_slots = [WP0, WP1]
        for tb in range(NTB_B):
            p.dma("gpsimd", "wgt", WP0[:, :], d_wp.ap()[0])
            p.dma("gpsimd", "wgt", WP1[:, :], d_wp.ap()[1])
            for m in range(DC):
                slot = wp_slots[m % 2]
                bank = pb[m % 2]
                for cc in range(DC):
                    p.op("tensor", lambda e, bank=bank, slot=slot, cc=cc, tb=tb:
                         e.matmul(bank[:, :], slot[:, cc * 128:(cc + 1) * 128],
                                  Yb[:, cc * T + tb * TBS: cc * T + (tb + 1) * TBS],
                                  start=(cc == 0), stop=(cc == DC - 1)),
                         reads=[slot, Yb], writes=[bank])
                p.op("vector", lambda e, bank=bank, m=m, tb=tb: e.scalar_tensor_tensor(
                    XR[:, m * TBS:(m + 1) * TBS], bank[:, :], BP[:, m:m + 1],
                    Xb[:, m * T + tb * TBS: m * T + (tb + 1) * TBS],
                    ALU.add, ALU.add),
                    reads=[bank, BP, Xb], writes=[XR])
                if m + 2 < DC:
                    p.dma("gpsimd", "wgt", wp_slots[m % 2][:, :], d_wp.ap()[m + 2])
            _ln(p, nc, pb, XR, XSQ0, XSQ1, TN, MROW, SROW, ALPHA, BETA,
                ONESC, ONES1, EPSC,
                dst=x2, dst_off=b * T + tb * TBS, dst_f32=None)

    if "q0" in tap_outs:
        _tapdump(p, nc, tap_outs["q0"], Qs)
    if "k0" in tap_outs:
        _tapdump(p, nc, tap_outs["k0"], Ks)
    if "v0" in tap_outs:
        _tapdump(p, nc, tap_outs["v0"], Vs)
    if "yb0" in tap_outs:
        _tapdump(p, nc, tap_outs["yb0"], Yb)

    p.fence()

    p.dma("sync", "cst", IDE[:, :], d_ident.ap())
    p.dma("gpsimd", "wgt", B2B[:, :], d_b2.ap())
    p.dma("gpsimd", "wgt", W1s[:, :], d_hw1.ap()[0])
    p.dma("gpsimd", "wgt", W3s[:, :], d_hw3.ap()[0])

    for tt in range(NTT):
        bank = pb[tt % 2]
        for dc in range(DC):
            p.op("tensor", lambda e, bank=bank, dc=dc, tt=tt: e.matmul(
                bank[:, 0:E],
                x2[:, dc * (BPC * T) + tt * 128: dc * (BPC * T) + (tt + 1) * 128],
                WGT[:, dc * E:(dc + 1) * E],
                start=(dc == 0), stop=(dc == DC - 1)),
                reads=[x2, WGT], writes=[bank])
        p.op("scalar", lambda e, bank=bank, tt=tt: e.activation(
            g_s[:, tt * E:(tt + 1) * E], bank[:, 0:E], AF.Sigmoid),
            reads=[bank], writes=[g_s])
    V4 = g_s[:, :].rearrange("p (t g j) -> p (t g) j", g=4, j=4)
    p.op("vector", lambda e: e.tensor_tensor(g_gs[:, :], V4[:, :, 0], V4[:, :, 1],
                                             ALU.max), reads=[g_s], writes=[g_gs])
    for j in (2, 3):
        p.op("vector", lambda e, j=j: e.tensor_tensor(g_gs[:, :], g_gs[:, :],
                                                      V4[:, :, j], ALU.max),
             reads=[g_s, g_gs], writes=[g_gs])
    G4 = g_gs[:, :].rearrange("p (t g) -> p t g", g=4)
    p.op("vector", lambda e: e.tensor_tensor(g_r1[:, :], G4[:, :, 0], G4[:, :, 1],
                                             ALU.max), reads=[g_gs], writes=[g_r1])
    p.op("vector", lambda e: e.tensor_tensor(g_r2[:, :], G4[:, :, 0], G4[:, :, 1],
                                             ALU.min), reads=[g_gs], writes=[g_r2])
    p.op("vector", lambda e: e.tensor_tensor(g_r3[:, :], G4[:, :, 2], G4[:, :, 3],
                                             ALU.max), reads=[g_gs], writes=[g_r3])
    p.op("vector", lambda e: e.tensor_tensor(g_r4[:, :], G4[:, :, 2], G4[:, :, 3],
                                             ALU.min), reads=[g_gs], writes=[g_r4])
    p.op("vector", lambda e: e.tensor_tensor(g_r1[:, :], g_r1[:, :], g_r3[:, :],
                                             ALU.min), reads=[g_r1, g_r3], writes=[g_r1])
    p.op("vector", lambda e: e.tensor_tensor(g_r2[:, :], g_r2[:, :], g_r4[:, :],
                                             ALU.max), reads=[g_r2, g_r4], writes=[g_r2])
    p.op("vector", lambda e: e.tensor_tensor(g_r1[:, :], g_r1[:, :], g_r2[:, :],
                                             ALU.max), reads=[g_r1, g_r2], writes=[g_r1])
    p.op("vector", lambda e: e.tensor_tensor(
        g_pen[:, :].rearrange("p (t g) -> p t g", g=4), G4,
        g_r1[:, :].unsqueeze(2).broadcast_to((128, NTT, 4)), ALU.is_ge),
        reads=[g_gs, g_r1], writes=[g_pen])
    p.op("vector", lambda e: e.tensor_scalar(g_pen[:, :], g_pen[:, :], -1.0,
                                             -NEG, ALU.add, ALU.mult),
         reads=[g_pen], writes=[g_pen])
    p.op("vector", lambda e: e.tensor_tensor(
        g_ms[:, :].rearrange("p (t g j) -> p t g j", g=4, j=4),
        g_s[:, :].rearrange("p (t g j) -> p t g j", g=4, j=4),
        g_pen[:, :].rearrange("p (t g) -> p t g", g=4).unsqueeze(3)
        .broadcast_to((128, NTT, 4, 4)), ALU.add),
        reads=[g_s, g_pen], writes=[g_ms])
    p.op("vector", lambda e: e.tensor_copy(g_ms0[:, :], g_ms[:, :]),
         reads=[g_ms], writes=[g_ms0])
    MS3 = g_ms[:, :].rearrange("p (t e) -> p t e", e=E)
    for _r in range(3):
        p.op("vector", lambda e: e.reduce_max(g_r2[:, :], MS3, AX.X),
             reads=[g_ms], writes=[g_r2])
        p.op("vector", lambda e: e.tensor_tensor(
            g_wu[:, :].rearrange("p (t e) -> p t e", e=E), MS3,
            g_r2[:, :].unsqueeze(2).broadcast_to((128, NTT, E)), ALU.is_ge),
            reads=[g_ms, g_r2], writes=[g_wu])
        p.op("vector", lambda e: e.tensor_scalar(g_wu[:, :], g_wu[:, :], -NEG,
                                                 None, ALU.mult),
             reads=[g_wu], writes=[g_wu])
        p.op("vector", lambda e: e.tensor_tensor(g_ms[:, :], g_ms[:, :],
                                                 g_wu[:, :], ALU.subtract),
             reads=[g_ms, g_wu], writes=[g_ms])
    p.op("vector", lambda e: e.reduce_max(g_r2[:, :], MS3, AX.X),
         reads=[g_ms], writes=[g_r2])
    p.op("vector", lambda e: e.tensor_tensor(
        g_wu[:, :].rearrange("p (t e) -> p t e", e=E),
        g_ms0[:, :].rearrange("p (t e) -> p t e", e=E),
        g_r2[:, :].unsqueeze(2).broadcast_to((128, NTT, E)), ALU.is_ge),
        reads=[g_ms0, g_r2], writes=[g_wu])
    p.op("vector", lambda e: e.tensor_tensor(g_wu[:, :], g_wu[:, :], g_s[:, :],
                                             ALU.mult),
         reads=[g_wu, g_s], writes=[g_wu])
    p.op("vector", lambda e: e.reduce_sum(
        g_r3[:, :], g_wu[:, :].rearrange("p (t e) -> p t e", e=E), AX.X),
        reads=[g_wu], writes=[g_r3])
    p.op("vector", lambda e: e.reciprocal(g_r4[:, :], g_r3[:, :]),
         reads=[g_r3], writes=[g_r4])
    p.op("vector", lambda e: e.tensor_scalar(g_r4[:, :], g_r4[:, :], ROUTE_SCALE,
                                             None, ALU.mult),
         reads=[g_r4], writes=[g_r4])
    p.op("vector", lambda e: e.tensor_tensor(
        g_we[:, :].rearrange("p (t e) -> p t e", e=E),
        g_wu[:, :].rearrange("p (t e) -> p t e", e=E),
        g_r4[:, :].unsqueeze(2).broadcast_to((128, NTT, E)), ALU.mult),
        reads=[g_wu, g_r4], writes=[g_we])
    for tt in range(NTT):
        p.op("tensor", lambda e, tt=tt: e.transpose(
            pb[2][0:E, 0:128], g_we[:, tt * E:(tt + 1) * E], IDE[:, :]),
            reads=[g_we, IDE], writes=[pb[2]])
        p.op("scalar", lambda e, tt=tt: e.copy(
            WE[:, tt * 128:(tt + 1) * 128], pb[2][0:E, 0:128]),
            reads=[pb[2]], writes=[WE])

    if "sall" in tap_outs:
        _tapdump(p, nc, tap_outs["sall"], g_s)
    if "we" in tap_outs:
        _tapdump(p, nc, tap_outs["we"], g_we)
    if "x2" in tap_outs:
        _tapdump(p, nc, tap_outs["x2"], x2)

    def ysl(dm, tb):
        t_ = (y_lo if dm < 4 else y_hi)
        off = (dm % 4) * (BPC * T) + tb * TBS
        return t_, t_[:, off:off + TBS]

    for tb in range(NTB):
        for dm in range(DC):
            bank = pb[(tb + dm) % 2]
            p.op("tensor", lambda e, bank=bank, dm=dm, tb=tb: e.matmul(
                bank[:, :], B2B[:, dm * 128:(dm + 1) * 128],
                WE[:, tb * TBS:(tb + 1) * TBS], start=True, stop=True),
                reads=[B2B, WE], writes=[bank])
            yt, ya = ysl(dm, tb)
            p.op("scalar", lambda e, bank=bank, ya=ya: e.copy(ya, bank[:, :]),
                 reads=[bank], writes=[yt])

    p.fence()

    p.dma("sync", "cst", OHS[:, :], d_oh.ap())
    p.dma("sync", "cst", B1T[:, :], d_b1t.ap())
    p.dma("sync", "cst", B3T[:, :], d_b3t.ap())

    hp = [HP0, HP0]
    for ex in range(E):
        p.dma("gpsimd", "wgt", W2s[:, :], d_hw2.ap()[ex])
        for tb in range(NTB):
            wrs = [WRS0, WRS1][tb % 2]
            wbank = pb[6 + tb % 2]
            p.op("tensor", lambda e, wbank=wbank, ex=ex, tb=tb: e.matmul(
                wbank[:, :], OHS[:, ex * 128:(ex + 1) * 128],
                WE[:, tb * TBS:(tb + 1) * TBS], start=True, stop=True),
                reads=[OHS, WE], writes=[wbank])
            p.op("scalar", lambda e, wbank=wbank, wrs=wrs: e.copy(wrs[:, :],
                                                                  wbank[:, :]),
                 reads=[wbank], writes=[wrs])
            hpt = hp[tb % 2]
            for im in range(DC):
                abank = pb[(im % 2) * 2]
                bbank = pb[1 + (im % 2) * 2]
                for dc in range(DC):
                    xsl = x2[:, dc * (BPC * T) + tb * TBS:
                             dc * (BPC * T) + (tb + 1) * TBS]
                    p.op("tensor", lambda e, abank=abank, im=im, dc=dc, xsl=xsl:
                         e.matmul(abank[:, :],
                                  W1s[:, im * 1024 + dc * 128: im * 1024 + (dc + 1) * 128],
                                  xsl, start=(dc == 0), stop=(dc == DC - 1)),
                         reads=[W1s, x2], writes=[abank])
                for dc in range(DC):
                    xsl = x2[:, dc * (BPC * T) + tb * TBS:
                             dc * (BPC * T) + (tb + 1) * TBS]
                    p.op("tensor", lambda e, bbank=bbank, im=im, dc=dc, xsl=xsl:
                         e.matmul(bbank[:, :],
                                  W3s[:, im * 1024 + dc * 128: im * 1024 + (dc + 1) * 128],
                                  xsl, start=(dc == 0), stop=(dc == DC - 1)),
                         reads=[W3s, x2], writes=[bbank])
                sa = [XSQ0, XSQ1][im % 2]
                p.op("scalar", lambda e, abank=abank, sa=sa, im=im, ex=ex:
                     e.activation(sa[:, :], abank[:, :], AF.Silu,
                                  bias=B1T[:, im * E + ex: im * E + ex + 1]),
                     reads=[abank, B1T], writes=[sa])
                p.op("vector", lambda e, bbank=bbank, sa=sa, im=im, ex=ex:
                     e.scalar_tensor_tensor(TN[:, :], bbank[:, :],
                                            B3T[:, im * E + ex: im * E + ex + 1],
                                            sa[:, :], ALU.add, ALU.mult),
                     reads=[bbank, B3T, sa], writes=[TN])
                p.op("vector", lambda e, hpt=hpt, im=im, wrs=wrs: e.tensor_tensor(
                    hpt[:, im * TBS:(im + 1) * TBS], TN[:, :], wrs[:, :], ALU.mult),
                    reads=[TN, wrs], writes=[hpt])
            if tb == NTB - 1 and ex + 1 < E:
                p.dma("gpsimd", "wgt", W1s[:, :], d_hw1.ap()[ex + 1])
                p.dma("gpsimd", "wgt", W3s[:, :], d_hw3.ap()[ex + 1])
            _w2pass(p, pb, W2s, hp[tb % 2], ysl, tb)

    if "yacc" in tap_outs:
        _tapdump2(p, nc, tap_outs["yacc"], y_lo, y_hi)

    p.fence()

    for tb in range(NTB):
        xo = [XO0, XO1][tb % 2]
        for cc in range(DC):
            yt, ya = ysl(cc, tb)
            p.op("vector", lambda e, cc=cc, tb=tb, ya=ya: e.tensor_tensor(
                XR[:, cc * TBS:(cc + 1) * TBS],
                x2[:, cc * (BPC * T) + tb * TBS: cc * (BPC * T) + (tb + 1) * TBS],
                ya, ALU.add),
                reads=[x2, yt], writes=[XR])
        _ln(p, nc, pb, XR, XSQ0, XSQ1, TN, MROW, SROW, ALPHA, BETA,
            ONESC, ONES1, EPSC, dst=None, dst_off=0, dst_f32=xo)
        bq = tb // NTB_B
        tloc = (tb % NTB_B) * TBS
        p.dma("sync", "out",
              d_out.ap()[bq].rearrange("(c p) t -> p c t", p=128)[:, :, tloc:tloc + TBS],
              xo[:, :].rearrange("p (c t) -> p c t", c=DC))

    p.wait_all("sync")
    p.wait_all("gpsimd")
    p.emit(nc)
    return nc


def _w2pass(p, pb, W2s, hpt, ysl, tb):
    for dm in range(DC):
        ybank = pb[4 + dm % 2]
        for ic in range(DC):
            p.op("tensor", lambda e, ybank=ybank, dm=dm, ic=ic, hpt=hpt: e.matmul(
                ybank[:, :],
                W2s[:, dm * 1024 + ic * 128: dm * 1024 + (ic + 1) * 128],
                hpt[:, ic * TBS:(ic + 1) * TBS],
                start=(ic == 0), stop=(ic == DC - 1)),
                reads=[W2s, hpt], writes=[ybank])
        yt, ya = ysl(dm, tb)
        p.op("vector", lambda e, ybank=ybank, ya=ya: e.tensor_tensor(
            ya, ya, ybank[:, :], ALU.add),
            reads=[yt, ybank], writes=[yt])


def _ln(p, nc, pb, XR, XSQ0, XSQ1, TN, MROW, SROW, ALPHA, BETA, ONESC, ONES1,
        EPSC, dst, dst_off, dst_f32):
    DCL = DC
    sbank, qbank, abank_, bbank_ = pb[2], pb[3], pb[4], pb[7]
    for cc in range(DCL):
        xsq = [XSQ0, XSQ1][cc % 2]
        p.op("scalar", lambda e, cc=cc, xsq=xsq: e.activation(
            xsq[:, :], XR[:, cc * TBS:(cc + 1) * TBS], AF.Square),
            reads=[XR], writes=[xsq])
        p.op("tensor", lambda e, cc=cc: e.matmul(
            sbank[0:1, :], ONESC[:, 0:1], XR[:, cc * TBS:(cc + 1) * TBS],
            start=(cc == 0), stop=(cc == DCL - 1)),
            reads=[ONESC, XR], writes=[sbank])
        p.op("tensor", lambda e, cc=cc, xsq=xsq: e.matmul(
            qbank[0:1, :], ONESC[:, 0:1], xsq[:, :],
            start=(cc == 0), stop=(cc == DCL - 1)),
            reads=[ONESC, xsq], writes=[qbank])
    inv = 1.0 / float(C)
    p.op("vector", lambda e: e.tensor_scalar(MROW[:, :], sbank[0:1, :], inv, None,
                                             ALU.mult),
         reads=[sbank], writes=[MROW])
    p.op("vector", lambda e: e.tensor_scalar(SROW[:, :], qbank[0:1, :], inv, None,
                                             ALU.mult),
         reads=[qbank], writes=[SROW])
    p.op("vector", lambda e: e.tensor_tensor(ALPHA[:, :], MROW[:, :], MROW[:, :],
                                             ALU.mult),
         reads=[MROW], writes=[ALPHA])
    p.op("vector", lambda e: e.tensor_tensor(SROW[:, :], SROW[:, :], ALPHA[:, :],
                                             ALU.subtract),
         reads=[SROW, ALPHA], writes=[SROW])
    p.op("scalar", lambda e: e.activation(BETA[:, :], SROW[:, :], AF.Ln,
                                          bias=EPSC[0:1, 0:1]),
         reads=[SROW, EPSC], writes=[BETA])
    p.op("scalar", lambda e: e.activation(ALPHA[:, :], BETA[:, :], AF.Exp,
                                          scale=-0.5),
         reads=[BETA], writes=[ALPHA])
    p.op("vector", lambda e: e.scalar_tensor_tensor(BETA[:, :], MROW[:, :], -1.0,
                                                    ALPHA[:, :], ALU.mult, ALU.mult),
         reads=[MROW, ALPHA], writes=[BETA])
    p.op("tensor", lambda e: e.matmul(abank_[:, :], ONES1[:, :], ALPHA[:, :],
                                      start=True, stop=True),
         reads=[ONES1, ALPHA], writes=[abank_])
    p.op("tensor", lambda e: e.matmul(bbank_[:, :], ONES1[:, :], BETA[:, :],
                                      start=True, stop=True),
         reads=[ONES1, BETA], writes=[bbank_])
    for cc in range(DCL):
        p.op("vector", lambda e, cc=cc: e.tensor_tensor(
            TN[:, :], XR[:, cc * TBS:(cc + 1) * TBS], abank_[:, :], ALU.mult),
            reads=[XR, abank_], writes=[TN])
        if dst is not None:
            dsl = dst[:, cc * (BPC * T) + dst_off: cc * (BPC * T) + dst_off + TBS]
            p.op("vector", lambda e, dsl=dsl: e.tensor_tensor(
                dsl, TN[:, :], bbank_[:, :], ALU.add),
                reads=[TN, bbank_], writes=[dst])
        else:
            p.op("vector", lambda e, cc=cc: e.tensor_tensor(
                dst_f32[:, cc * TBS:(cc + 1) * TBS], TN[:, :], bbank_[:, :],
                ALU.add),
                reads=[TN, bbank_], writes=[dst_f32])


def _tapdump(p, nc, dram, sb):
    p.dma("gpsimd", "tap", dram.ap(), sb[:, :])


def _tapdump2(p, nc, dram, lo, hi):
    n = lo.shape[1]
    p.dma("gpsimd", "tap", dram.ap()[:, 0:n], lo[:, :])
    p.dma("gpsimd", "tap", dram.ap()[:, n:2 * n], hi[:, :])



def _run(inputs, taps=()):
    shared = _prep_shared(inputs)
    x = np.asarray(inputs["L0_attn_x"], np.float32)
    nc = build(taps=taps)
    in_maps = []
    for c in range(NCORES):
        m = dict(shared)
        m["xcm"] = np.ascontiguousarray(
            x[c * BPC:(c + 1) * BPC].transpose(0, 2, 1))
        in_maps.append(m)
    res = run_bass_kernel_spmd(nc, in_maps, core_ids=list(range(NCORES)))
    outs = []
    for c in range(NCORES):
        o = res.results[c]["out"]
        outs.append(o)
    full = np.concatenate(outs, axis=0)
    return np.ascontiguousarray(full.transpose(0, 2, 1)), res


def kernel(**inputs):
    out, _ = _run(inputs, taps=_TAPS)
    return out


# version 07 (final)
# speedup vs baseline: 1.2765x; 1.2765x over previous
import numpy as np
import ml_dtypes
import concourse.bass as bass
import concourse.mybir as mybir
from concourse.bass_utils import run_bass_kernel_spmd

F32 = mybir.dt.float32
BF16 = mybir.dt.bfloat16
AF = mybir.ActivationFunctionType
ALU = mybir.AluOpType
AX = mybir.AxisListType

BF = ml_dtypes.bfloat16

B, T, C = 16, 2048, 1024
H, D = 16, 64
NCORES = 8
BPC = B // NCORES
E, I = 16, 1024
NHG, HPG = 4, 4
DC = C // 128
TBS = 512
NTB_B = T // TBS
NTB = BPC * NTB_B
NTT = (BPC * T) // 128
ROUTE_SCALE = 2.5
LN_EPS = 1e-5
EXP_SHIFT = -8.0
NEG = -1.0e30

_SLOPE0 = 2.0 ** (-0.5)
SLOPES = [_SLOPE0 * _SLOPE0 ** i for i in range(H)]

WINDOW_LOGCUT = 100.0
ENABLE_WINDOW = False

_TAPS = ()


ENGS = ("tensor", "vector", "scalar", "gpsimd", "sync")


def _rn(x):
    if isinstance(x, str):
        return x
    t = getattr(x, "tensor", None)
    return t.name if t is not None else x.name


SELF_SYNC = ("vector", "scalar", "gpsimd")


class P:
    def __init__(self):
        self.items = {e: [] for e in ENGS}
        self.cnt = {e: 0 for e in ENGS}
        self.dma_cnt = {}
        self.vc = {e: {e: 0} for e in ENGS}
        self.hist = {e: {0: {e: 0}} for e in ENGS}
        self.res = {}
        self.self_waited = {e: 0 for e in ENGS}
        self.n_waits = 0

    def _deps(self, reads, writes):
        d = []
        for r in reads:
            ent = self.res.get(r)
            if ent and ent[0]:
                d.append(ent[0])
        for w in writes:
            ent = self.res.get(w)
            if ent:
                if ent[0]:
                    d.append(ent[0])
                d.extend(ent[1].items())
        return d

    def _upd(self, src, cnt, reads, writes):
        for r in reads:
            self.res.setdefault(r, [None, {}])[1][src] = cnt
        for w in writes:
            self.res[w] = [(src, cnt), {}]

    def _resolve(self, eng, deps):
        waits = []
        vc = self.vc[eng]
        for src, cnt in sorted(set(deps)):
            if src == eng:
                if eng in SELF_SYNC and cnt > self.self_waited[eng]:
                    waits.append((eng, cnt))
                    self.self_waited[eng] = cnt
                continue
            if vc.get(src, 0) >= cnt:
                continue
            waits.append((src, cnt))
            for s2, c2 in self.hist[src][cnt].items():
                if vc.get(s2, 0) < c2:
                    vc[s2] = c2
        self.n_waits += len(waits)
        return waits

    def _adv(self, src):
        self.cnt[src] = self.cnt.get(src, 0) + 1
        c = self.cnt[src]
        self.vc[src][src] = c
        self.hist[src][c] = dict(self.vc[src])
        return c

    def op(self, eng, fn, reads=(), writes=(), lp=False):
        reads = [_rn(x) for x in reads]
        writes = [_rn(x) for x in writes]
        waits = self._resolve(eng, self._deps(reads, writes))
        c = self._adv(eng)
        self._upd(eng, c, reads, writes)
        self.items[eng].append(("op", waits, fn, lp))
        return (eng, c)

    def dma(self, eng, stream, out, in_, reads_extra=(), **kw):
        r_in = [_rn(in_)] + [_rn(x) for x in reads_extra]
        w_out = [_rn(out)]
        ps = "dma:" + stream
        if ps not in self.vc:
            self.vc[ps] = {ps: 0}
            self.hist[ps] = {0: {ps: 0}}
            self.cnt[ps] = 0
            self.dma_cnt[stream] = 0
        waits = self._resolve(eng, self._deps(r_in, w_out))
        for s, c in self.vc[eng].items():
            if self.vc[ps].get(s, 0) < c:
                self.vc[ps][s] = c
        k = self._adv(ps)
        self.dma_cnt[stream] = k
        self._upd(ps, k, r_in, w_out)
        self.items[eng].append(("dma", waits, stream, out, in_, kw))
        return (ps, k)

    def wait_all(self, eng):
        deps = [(s, c) for s, c in self.cnt.items() if s != eng and c > 0]
        waits = self._resolve(eng, deps)
        if waits:
            self.items[eng].append(("waitonly", waits))

    def fence(self):
        for e in ENGS:
            self.wait_all(e)

    def emit(self, nc):
        sems = {e: nc.alloc_semaphore("sem_" + e) for e in ENGS}
        for s in self.dma_cnt:
            sems["dma:" + s] = nc.alloc_semaphore("semd_" + s)

        def wv(src, cnt):
            return 16 * cnt if src.startswith("dma:") else cnt

        def mk(eng_name):
            def run(engine):
                for it in self.items[eng_name]:
                    kind = it[0]
                    for (src, cnt) in it[1]:
                        engine.wait_ge(sems[src], wv(src, cnt))
                    if kind == "op":
                        if it[3]:
                            with nc.allow_low_precision(reason="deliberate bf16"):
                                it[2](engine).then_inc(sems[eng_name], 1)
                        else:
                            it[2](engine).then_inc(sems[eng_name], 1)
                    elif kind == "dma":
                        engine.dma_start(out=it[3], in_=it[4], **it[5]).then_inc(
                            sems["dma:" + it[2]], 16
                        )
            return run

        with nc.Block() as block:
            for e in ENGS:
                if self.items[e]:
                    getattr(block, e)(mk(e))
        return sems



def _split3(v):
    t1 = v.astype(BF).astype(np.float32)
    r = v - t1
    t2 = r.astype(BF).astype(np.float32)
    t3 = (r - t2).astype(BF).astype(np.float32)
    return t1.astype(BF), t2.astype(BF), t3.astype(BF)


def _prep_shared(inp):
    W_attn = np.asarray(inp["L0_attn_W_attn"], np.float32)
    W_proj = np.asarray(inp["L0_attn_W_proj"], np.float32)
    b_attn = np.asarray(inp["L0_attn_b_attn"], np.float32)
    b_proj = np.asarray(inp["L0_attn_b_proj"], np.float32)
    temp = np.asarray(inp["L0_attn_temperature"], np.float32).reshape(1, 1)
    Wg = np.asarray(inp["L0_ffn_Wg"], np.float32)
    W1 = np.asarray(inp["L0_ffn_W1"], np.float32)
    b1 = np.asarray(inp["L0_ffn_b1"], np.float32)
    W2 = np.asarray(inp["L0_ffn_W2"], np.float32)
    b2 = np.asarray(inp["L0_ffn_b2"], np.float32)
    W3 = np.asarray(inp["L0_ffn_W3"], np.float32)
    b3 = np.asarray(inp["L0_ffn_b3"], np.float32)

    d = {}
    wqk = np.empty((NHG, 4, 128, DC * 128), np.float32)
    for hg in range(NHG):
        for m in range(4):
            col0 = (0 if m < 2 else C) + hg * 256 + (m % 2) * 128
            blk = W_attn[:, col0:col0 + 128]
            wqk[hg, m] = blk.reshape(DC, 128, 128).transpose(1, 0, 2).reshape(128, -1)
    d["wqk"] = wqk
    wv = np.empty((NHG, 128, DC * 256), np.float32)
    for hg in range(NHG):
        blk = W_attn[:, 2 * C + hg * 256: 2 * C + (hg + 1) * 256]
        wv[hg] = blk.reshape(DC, 128, 256).transpose(1, 0, 2).reshape(128, -1)
    d["wv"] = wv
    wp = np.empty((DC, 128, DC * 128), np.float32)
    for m in range(DC):
        blk = W_proj[:, m * 128:(m + 1) * 128]
        wp[m] = blk.reshape(DC, 128, 128).transpose(1, 0, 2).reshape(128, -1)
    d["wp"] = wp
    d["hw1"] = np.ascontiguousarray(
        W1.reshape(E, 8, 128, 8, 128).transpose(0, 4, 1, 3, 2).reshape(E, 128, 8192))
    d["hw3"] = np.ascontiguousarray(
        W3.reshape(E, 8, 128, 8, 128).transpose(0, 4, 1, 3, 2).reshape(E, 128, 8192))
    d["hw2"] = np.ascontiguousarray(
        W2.reshape(E, 8, 128, 8, 128).transpose(0, 4, 1, 3, 2).reshape(E, 128, 8192))
    d["wgt"] = np.ascontiguousarray(
        Wg.T.reshape(DC, 128, E).transpose(1, 0, 2).reshape(128, DC * E))
    d["batt"] = np.ascontiguousarray(b_attn.reshape(24, 128).T)
    d["bav"] = np.ascontiguousarray(b_attn[2 * C:].reshape(1, C))
    d["bp"] = np.ascontiguousarray(b_proj.reshape(8, 128).T)
    d["b1t"] = np.ascontiguousarray(
        b1.T.reshape(8, 128, E).transpose(1, 0, 2).reshape(128, 8 * E))
    d["b3t"] = np.ascontiguousarray(
        b3.T.reshape(8, 128, E).transpose(1, 0, 2).reshape(128, 8 * E))
    d["b2"] = np.ascontiguousarray(b2)
    d["temp"] = temp
    return d


TABW = 2432


def _consts():
    c = {}
    c["ident"] = np.eye(128, dtype=np.float32)
    ii = np.arange(128)[:, None]
    ww = np.arange(TABW)[None, :]
    rel = (ii - ww + 384).astype(np.float32)
    c["tab"] = np.where(ii > ww - 384, -1.0e33, rel).astype(np.float32)
    oh = np.zeros((E, E * 128), np.float32)
    for e in range(E):
        oh[e, e * 128:(e + 1) * 128] = 1.0
    c["oh"] = oh.astype(BF)
    c["ones1"] = np.ones((1, 128), np.float32)
    c["eps"] = np.full((1, 1), LN_EPS, np.float32)
    c["ones64"] = np.ones((128, 64), np.float32).astype(BF)
    c["imms"] = np.full((128, 1), EXP_SHIFT, np.float32)
    return c



def _kcs(h, qb):
    kmax = 4 * qb + 3
    kmin = 0
    if ENABLE_WINDOW:
        lim = qb * TBS - WINDOW_LOGCUT / SLOPES[h]
        kmin = max(0, int(np.ceil((lim - 127.0) / 128.0)))
        kmin = min(kmin, kmax)
    return list(range(kmin, kmax + 1))


def build(taps=()):
    nc = bass.Bass("TRN2", target_bir_lowering=False)

    d_x = nc.dram_tensor("xcm", [BPC, C, T], F32, kind="ExternalInput")
    d_wqk = nc.dram_tensor("wqk", [NHG, 4, 128, DC * 128], F32, kind="ExternalInput")
    d_wv = nc.dram_tensor("wv", [NHG, 128, DC * 256], F32, kind="ExternalInput")
    d_wp = nc.dram_tensor("wp", [DC, 128, DC * 128], F32, kind="ExternalInput")
    d_hw1 = nc.dram_tensor("hw1", [E, 128, 8192], F32, kind="ExternalInput")
    d_hw3 = nc.dram_tensor("hw3", [E, 128, 8192], F32, kind="ExternalInput")
    d_hw2 = nc.dram_tensor("hw2", [E, 128, 8192], F32, kind="ExternalInput")
    d_wgt = nc.dram_tensor("wgt", [128, DC * E], F32, kind="ExternalInput")
    d_batt = nc.dram_tensor("batt", [128, 24], F32, kind="ExternalInput")
    d_bav = nc.dram_tensor("bav", [1, C], F32, kind="ExternalInput")
    d_bp = nc.dram_tensor("bp", [128, 8], F32, kind="ExternalInput")
    d_b1t = nc.dram_tensor("b1t", [128, 8 * E], F32, kind="ExternalInput")
    d_b3t = nc.dram_tensor("b3t", [128, 8 * E], F32, kind="ExternalInput")
    d_b2 = nc.dram_tensor("b2", [E, I], F32, kind="ExternalInput")
    d_temp = nc.dram_tensor("temp", [1, 1], F32, kind="ExternalInput")
    d_out = nc.dram_tensor("out", [BPC, C, T], F32, kind="ExternalOutput")

    cns = _consts()
    d_ident = nc.inline_tensor(cns["ident"], "c_ident")
    d_tab = nc.inline_tensor(cns["tab"], "c_tab")
    d_oh = nc.inline_tensor(np.asarray(cns["oh"]), "c_oh")
    d_imms = nc.inline_tensor(cns["imms"], "c_imms")
    d_ones1 = nc.inline_tensor(cns["ones1"], "c_ones1")
    d_ones64 = nc.inline_tensor(np.asarray(cns["ones64"]), "c_ones64")
    d_eps = nc.inline_tensor(cns["eps"], "c_eps")

    tap_outs = {}
    for t_ in taps:
        shp = {"x2": [128, DC * BPC * T], "we": [128, NTT * E],
               "yacc": [128, DC * BPC * T], "yb0": [128, DC * T],
               "q0": [128, 2 * T], "k0": [128, 2 * T], "v0": [128, 16 * 256],
               "sall": [128, NTT * E]}[t_]
        tap_outs[t_] = nc.dram_tensor("tap_" + t_, shp, F32, kind="ExternalOutput")

    guard = nc.alloc_sbuf_tensor("ringguard", [128, 256], mybir.dt.uint8)
    gml = nc.lookup_mloc(guard)
    gaddr = gml.addr() if callable(gml.addr) else gml.addr
    base = (int(gaddr) + 256 + 256 + 31) & ~31
    KB = 1024

    def at(name, shape, dtype, off):
        t = nc.alloc_sbuf_tensor_at(name, shape, dtype, offset=base + off)
        return t

    x2 = at("x2", [128, DC * BPC * T], BF16, 0)
    o_xb = 64 * KB
    Xb = at("Xb", [128, DC * T], BF16, o_xb)
    y_lo = at("y_lo", [128, 4 * BPC * T], BF16, o_xb)
    o_yb = 96 * KB
    Yb = at("Yb", [128, DC * T], BF16, o_yb)
    y_hi = at("y_hi", [128, 4 * BPC * T], BF16, o_yb)
    o_q = 128 * KB
    Qs = at("Qs", [128, 2 * T], BF16, o_q)
    Ks = at("Ks", [128, 2 * T], BF16, o_q + 8 * KB)
    HP0 = at("HP0", [128, 8 * TBS], BF16, o_q)
    g_s = at("g_s", [128, NTT * E], F32, o_q)
    g_ms = at("g_ms", [128, NTT * E], F32, o_q + 2 * KB)
    g_ms0 = at("g_ms0", [128, NTT * E], F32, o_q + 4 * KB)
    g_we = at("g_we", [128, NTT * E], F32, o_q + 6 * KB)
    g_gs = at("g_gs", [128, NTT * 4], F32, o_q + 8 * KB)
    g_r1 = at("g_r1", [128, NTT], F32, o_q + 8 * KB + 512)
    g_r2 = at("g_r2", [128, NTT], F32, o_q + 8 * KB + 640)
    g_r3 = at("g_r3", [128, NTT], F32, o_q + 8 * KB + 768)
    g_r4 = at("g_r4", [128, NTT], F32, o_q + 8 * KB + 896)
    g_pen = at("g_pen", [128, NTT * 4], F32, o_q + 10 * KB)
    g_wu = at("g_wu", [128, NTT * E], F32, o_q + 11 * KB)
    o_v = 144 * KB
    Vs = at("Vs", [128, 16 * 260], BF16, o_v)
    WE = at("WE", [E, NTT * 128], BF16, o_v)
    o_ar = 152 * KB
    W1s = at("W1s", [128, 8192], BF16, o_ar)
    W3s = at("W3s", [128, 8192], BF16, o_ar + 16 * KB)
    W2s = at("W2s", [128, 8192], BF16, o_ar + 32 * KB)
    WRS0 = at("WRS0", [128, TBS], BF16, o_ar + 48 * KB)
    WRS1 = at("WRS1", [128, TBS], BF16, o_ar + 49 * KB)
    XO0 = at("XO0", [128, DC * TBS], F32, o_ar)
    XO1 = at("XO1", [128, DC * TBS], F32, o_ar + 16 * KB)
    a = o_ar + 512
    WQK0 = at("WQK0", [128, DC * 128], BF16, a); a += 2 * KB
    WQK1 = at("WQK1", [128, DC * 128], BF16, a); a += 2 * KB
    WVs = at("WVs", [128, DC * 256], BF16, a); a += 4 * KB
    PT0 = at("PT0", [128, TBS], BF16, a); a += KB
    PT1 = at("PT1", [128, TBS], BF16, a); a += KB
    PT2 = at("PT2", [128, TBS], BF16, a); a += KB
    LNT = at("LNT", [65, TBS], F32, a); a += 2 * KB
    LINV = at("LINV", [128, TBS], BF16, a); a += KB
    LRS = at("LRS", [64, TBS], BF16, a); a += KB
    BRV = at("BRV", [128, C], BF16, a); a += 2 * KB
    BAV = at("BAV", [1, C], F32, o_ar + 32 * KB)
    IMMS = at("IMMS", [128, 1], F32, a); a += 32
    WP0 = at("WP0", [128, DC * 128], BF16, a); a += 2 * KB
    WP1 = at("WP1", [128, DC * 128], BF16, a); a += 2 * KB
    TAB = at("TAB", [128, TABW], F32, a); a += 9728
    assert a - o_ar <= 32 * KB, a - o_ar
    XR = at("XR", [128, DC * TBS], BF16, o_ar + 32 * KB)
    MROW = at("MROW", [1, TBS], F32, o_ar + 40 * KB)
    SROW = at("SROW", [1, TBS], F32, o_ar + 42 * KB)
    ALPHA = at("ALPHA", [1, TBS], F32, o_ar + 44 * KB)
    BETA = at("BETA", [1, TBS], F32, o_ar + 46 * KB)

    s = 202 * KB
    IDE = at("IDE", [128, 128], F32, o_q + 13 * KB + 512)
    ONES1 = at("ONES1", [1, 128], F32, s); s += 512
    ONES64 = at("ONES64", [128, 64], BF16, s); s += 128
    ONESC = ONES64
    BATT = at("BATT", [128, 24], F32, s); s += 96
    BQS = at("BQS", [128, 8], F32, s); s += 32
    BP = at("BP", [128, 8], F32, s); s += 32
    INV8T = at("INV8T", [128, 1], F32, s); s += 32
    TEMP = at("TEMP", [1, 1], F32, s); s += 32
    TROW = at("TROW", [1, 1], F32, s); s += 32
    TROW2 = at("TROW2", [1, 1], F32, s); s += 32
    EPSC = at("EPSC", [1, 1], F32, s); s += 32
    WGT = at("WGT", [128, DC * E], BF16, s); s += 256
    B1T = at("B1T", [128, 8 * E], F32, s); s += 512
    B3T = at("B3T", [128, 8 * E], F32, s); s += 512
    B2B = at("B2B", [E, I], BF16, o_q + 14 * KB)
    OHS = at("OHS", [E, E * 128], BF16, o_q + 8 * KB)
    TN = at("TN", [128, TBS], BF16, s); s += KB
    XSQ0 = at("XSQ0", [128, TBS], BF16, s); s += KB
    XSQ1 = at("XSQ1", [128, TBS], BF16, s); s += KB
    assert base + s <= nc.sbuf_top, (base, s, nc.sbuf_top)

    pb = [nc.alloc_psum_tensor(f"pb{i}", [128, TBS], F32) for i in range(8)]

    p = P()

    p.dma("sync", "cst", IMMS[:, :], d_imms.ap())
    p.dma("sync", "cst", TAB[:, :], d_tab.ap())
    p.dma("gpsimd", "wgt", WGT[:, :], d_wgt.ap())
    p.dma("sync", "cst", BATT[:, :], d_batt.ap())
    p.dma("sync", "cst", BAV[:, :], d_bav.ap())
    p.dma("sync", "cst", BP[:, :], d_bp.ap())
    p.dma("sync", "cst", TEMP[:, :], d_temp.ap())
    p.dma("sync", "cst", ONES1[:, :], d_ones1.ap())
    p.dma("sync", "cst", ONES64[:, :], d_ones64.ap())
    p.dma("sync", "cst", EPSC[:, :], d_eps.ap())
    p.op("vector", lambda e: e.tensor_scalar(TROW[:, :], TEMP[:, :], 8.0, None,
                                             ALU.mult),
         reads=[TEMP], writes=[TROW])
    p.op("vector", lambda e: e.reciprocal(TROW2[:, :], TROW[:, :]),
         reads=[TROW], writes=[TROW2])
    p.op("tensor", lambda e: e.matmul(pb[0][:, 0:1], ONES1[:, :], TROW2[:, :],
                                      start=True, stop=True),
         reads=[ONES1, TROW2], writes=[pb[0]])
    p.op("scalar", lambda e: e.copy(INV8T[:, :], pb[0][:, 0:1]),
         reads=[pb[0]], writes=[INV8T])
    for i in range(8):
        p.op("vector", lambda e, i=i: e.tensor_tensor(
            BQS[:, i:i + 1], BATT[:, i:i + 1], INV8T[:, :], ALU.mult),
            reads=[BATT, INV8T], writes=[BQS])
    for half in range(2):
        p.op("tensor", lambda e, half=half: e.matmul(
            pb[1][:, :], ONES1[:, :], BAV[:, half * TBS:(half + 1) * TBS],
            start=True, stop=True),
            reads=[ONES1, BAV], writes=[pb[1]])
        p.op("scalar", lambda e, half=half: e.copy(
            BRV[:, half * TBS:(half + 1) * TBS], pb[1][:, :]),
            reads=[pb[1]], writes=[BRV])
    p.op("vector", lambda e: e.tensor_copy(
        Vs[:, :].rearrange("p (x e) -> p x e", e=65)[:, :, 64:65],
        ONES64[:, 0:64].unsqueeze(2)),
        reads=[ONES64], writes=[Vs])
    p.fence()

    wqk_slots = [WQK0, WQK1]

    def hg_prefetch(hg2):
        p.dma("gpsimd", "wgt", wqk_slots[0][:, :], d_wqk.ap()[hg2, 0])
        p.dma("gpsimd", "wgt", wqk_slots[1][:, :], d_wqk.ap()[hg2, 1])
        p.dma("gpsimd", "wgt", WVs[:, :], d_wv.ap()[hg2])

    hg_prefetch(0)
    for b in range(BPC):
        p.dma("gpsimd", "wgt", Xb[:, :].rearrange("p (c t) -> p c t", c=DC),
              d_x.ap()[b].rearrange("(c p) t -> p c t", p=128))
        pend = [None]

        for hg in range(NHG):
            for m in range(4):
                slot = wqk_slots[m % 2]
                dst = Qs if m < 2 else Ks
                fc = m if m < 2 else m - 2
                for tb in range(NTB_B):
                    bank = pb[tb % 2]
                    for cc in range(DC):
                        p.op("tensor", lambda e, bank=bank, slot=slot, cc=cc, tb=tb:
                             e.matmul(bank[:, :], slot[:, cc * 128:(cc + 1) * 128],
                                      Xb[:, cc * T + tb * TBS: cc * T + (tb + 1) * TBS],
                                      start=(cc == 0), stop=(cc == DC - 1)),
                             reads=[slot, Xb], writes=[bank])
                    if m < 2:
                        qi = 2 * hg + m
                        p.op("scalar", lambda e, bank=bank, dst=dst, fc=fc, tb=tb, qi=qi:
                             e.activation(dst[:, fc * T + tb * TBS: fc * T + (tb + 1) * TBS],
                                          bank[:, :], AF.Identity,
                                          bias=BQS[:, qi:qi + 1], scale=INV8T[:, 0:1]),
                             reads=[bank, BQS, INV8T], writes=[dst])
                    else:
                        ki = 8 + 2 * hg + (m - 2)
                        p.op("scalar", lambda e, bank=bank, dst=dst, fc=fc, tb=tb, ki=ki:
                             e.activation(dst[:, fc * T + tb * TBS: fc * T + (tb + 1) * TBS],
                                          bank[:, :], AF.Identity,
                                          bias=BATT[:, ki:ki + 1]),
                             reads=[bank, BATT], writes=[dst])
                if m + 2 < 4:
                    p.dma("gpsimd", "wgt", wqk_slots[(m + 2) % 2][:, :],
                          d_wqk.ap()[hg, m + 2])
            for tt in range(16):
                bank = pb[tt % 2]
                for cc in range(DC):
                    p.op("tensor", lambda e, bank=bank, cc=cc, tt=tt: e.matmul(
                        bank[:, 0:256],
                        Xb[:, cc * T + tt * 128: cc * T + (tt + 1) * 128],
                        WVs[:, cc * 256:(cc + 1) * 256],
                        start=(cc == 0), stop=(cc == DC - 1)),
                        reads=[Xb, WVs], writes=[bank])
                p.op("vector", lambda e, bank=bank, tt=tt, hg=hg: e.tensor_tensor(
                    Vs[:, tt * 260:tt * 260 + 260]
                    .rearrange("p (h v) -> p h v", v=65)[:, :, 0:64],
                    bank[:, 0:256].rearrange("p (h d) -> p h d", d=64),
                    BRV[:, hg * 256:(hg + 1) * 256]
                    .rearrange("p (h d) -> p h d", d=64), ALU.add),
                    reads=[bank, BRV], writes=[Vs])
            if b * NHG + hg + 1 < BPC * NHG:
                hg_prefetch((hg + 1) % NHG)

            for hp_ in range(2):
                hA = hg * HPG + 2 * hp_
                hB = hA + 1
                fc = hp_
                slA, slB = SLOPES[hA], SLOPES[hB]
                for qb in range(NTB_B):
                    kcs = _kcs(hA, qb)
                    ybkA, ybkB = pb[5], pb[6]
                    ptsA = [PT0, PT1]
                    ptsB = [PT2, XSQ0]
                    nk = len(kcs)

                    def s_pair(i):
                        kc = kcs[i]
                        sbA = [pb[2], pb[3]][i % 2]
                        sbB = [pb[4], pb[0]][i % 2]
                        wo = 384 - 128 * (kc - 4 * qb)
                        for (sb, rlo, tp) in ((sbA, 0, (0, 0)),
                                              (sbB, 64, (64, 0))):
                            p.op("tensor", lambda e, sb=sb, kc=kc, qb=qb,
                                 fc=fc, rlo=rlo, tp=tp:
                                 e.matmul(sb[:, :],
                                          Ks[rlo:rlo + 64, fc * T + kc * 128: fc * T + (kc + 1) * 128],
                                          Qs[rlo:rlo + 64, fc * T + qb * TBS: fc * T + (qb + 1) * TBS],
                                          start=True, stop=True,
                                          tile_position=tp),
                                 reads=[Ks, Qs], writes=[sb])
                        for (sb, sl, pt) in ((sbA, slA, ptsA[i % 2]),
                                             (sbB, slB, ptsB[i % 2])):
                            p.op("vector", lambda e, sb=sb, wo=wo, sl=sl:
                                 e.scalar_tensor_tensor(sb[:, :],
                                                        TAB[:, wo:wo + TBS],
                                                        sl, sb[:, :],
                                                        ALU.mult, ALU.add),
                                 reads=[sb, TAB], writes=[sb])
                            p.op("scalar", lambda e, sb=sb, pt=pt:
                                 e.activation(pt[:, :], sb[:, :], AF.Exp,
                                              bias=IMMS[:, 0:1]),
                                 reads=[sb, IMMS], writes=[pt])

                    def pv_pair(i):
                        kc = kcs[i]
                        st, sp = (i == 0), (i == nk - 1)
                        for (hh, pt, ybk) in ((2 * hp_, ptsA[i % 2], ybkA),
                                              (2 * hp_ + 1, ptsB[i % 2], ybkB)):
                            p.op("tensor", lambda e, pt=pt, kc=kc, hh=hh,
                                 st=st, sp=sp, ybk=ybk:
                                 e.matmul(ybk[0:65, :],
                                          Vs[:, kc * 260 + hh * 65: kc * 260 + hh * 65 + 65],
                                          pt[:, :], start=st, stop=sp),
                                 reads=[Vs, pt], writes=[ybk])

                    for i in range(nk):
                        s_pair(i)
                        if i == 1 and pend[0] is not None:
                            pend[0]()
                            pend[0] = None
                        if i >= 1:
                            pv_pair(i - 1)
                    pv_pair(nk - 1)

                    def mk_tail(qb=qb, hA=hA, ybkA=ybkA, ybkB=ybkB):
                        def tail():
                            for (h, ybank) in ((hA, ybkA), (hA + 1, ybkB)):
                                p.op("scalar", lambda e, ybank=ybank:
                                     e.activation(LNT[64:65, :],
                                                  ybank[64:65, :], AF.Ln),
                                     reads=[ybank], writes=[LNT])
                                p.op("scalar", lambda e: e.activation(
                                    LINV[64:65, :], LNT[64:65, :], AF.Exp,
                                    scale=-1.0),
                                    reads=[LNT], writes=[LINV], lp=True)
                                p.op("tensor", lambda e: e.matmul(
                                    pb[7][0:64, :], ONES64[64:65, :],
                                    LINV[64:65, :],
                                    start=True, stop=True,
                                    tile_position=(64, 0)),
                                    reads=[ONES64, LINV], writes=[pb[7]])
                                p.op("scalar", lambda e: e.copy(
                                    LRS[:, :], pb[7][0:64, :]),
                                    reads=[pb[7]], writes=[LRS])
                                yoff = (h // 2) * T + qb * TBS
                                ypar = (h % 2) * 64
                                p.op("vector", lambda e, ybank=ybank,
                                     yoff=yoff, ypar=ypar: e.tensor_tensor(
                                     Yb[ypar:ypar + 64, yoff:yoff + TBS],
                                     ybank[0:64, :], LRS[:, :], ALU.mult),
                                     reads=[ybank, LRS], writes=[Yb])
                        return tail
                    pend[0] = mk_tail()

        if pend[0] is not None:
            pend[0]()
            pend[0] = None

        wp_slots = [WP0, WP1]
        for tb in range(NTB_B):
            p.dma("gpsimd", "wgt", WP0[:, :], d_wp.ap()[0])
            p.dma("gpsimd", "wgt", WP1[:, :], d_wp.ap()[1])
            for m in range(DC):
                slot = wp_slots[m % 2]
                bank = pb[m % 2]
                for cc in range(DC):
                    p.op("tensor", lambda e, bank=bank, slot=slot, cc=cc, tb=tb:
                         e.matmul(bank[:, :], slot[:, cc * 128:(cc + 1) * 128],
                                  Yb[:, cc * T + tb * TBS: cc * T + (tb + 1) * TBS],
                                  start=(cc == 0), stop=(cc == DC - 1)),
                         reads=[slot, Yb], writes=[bank])
                p.op("vector", lambda e, bank=bank, m=m, tb=tb: e.scalar_tensor_tensor(
                    XR[:, m * TBS:(m + 1) * TBS], bank[:, :], BP[:, m:m + 1],
                    Xb[:, m * T + tb * TBS: m * T + (tb + 1) * TBS],
                    ALU.add, ALU.add),
                    reads=[bank, BP, Xb], writes=[XR])
                if m + 2 < DC:
                    p.dma("gpsimd", "wgt", wp_slots[m % 2][:, :], d_wp.ap()[m + 2])
            _ln(p, nc, pb, XR, XSQ0, XSQ1, TN, MROW, SROW, ALPHA, BETA,
                ONESC, ONES1, EPSC,
                dst=x2, dst_off=b * T + tb * TBS, dst_f32=None)

    if "q0" in tap_outs:
        _tapdump(p, nc, tap_outs["q0"], Qs)
    if "k0" in tap_outs:
        _tapdump(p, nc, tap_outs["k0"], Ks)
    if "v0" in tap_outs:
        _tapdump(p, nc, tap_outs["v0"], Vs)
    if "yb0" in tap_outs:
        _tapdump(p, nc, tap_outs["yb0"], Yb)

    p.fence()

    p.dma("sync", "cst", IDE[:, :], d_ident.ap())
    p.dma("gpsimd", "wgt", B2B[:, :], d_b2.ap())
    p.dma("gpsimd", "wgt", W1s[:, :], d_hw1.ap()[0])
    p.dma("gpsimd", "wgt", W3s[:, :], d_hw3.ap()[0])

    for tt in range(NTT):
        bank = pb[tt % 2]
        for dc in range(DC):
            p.op("tensor", lambda e, bank=bank, dc=dc, tt=tt: e.matmul(
                bank[:, 0:E],
                x2[:, dc * (BPC * T) + tt * 128: dc * (BPC * T) + (tt + 1) * 128],
                WGT[:, dc * E:(dc + 1) * E],
                start=(dc == 0), stop=(dc == DC - 1)),
                reads=[x2, WGT], writes=[bank])
        p.op("scalar", lambda e, bank=bank, tt=tt: e.activation(
            g_s[:, tt * E:(tt + 1) * E], bank[:, 0:E], AF.Sigmoid),
            reads=[bank], writes=[g_s])
    V4 = g_s[:, :].rearrange("p (t g j) -> p (t g) j", g=4, j=4)
    p.op("vector", lambda e: e.tensor_tensor(g_gs[:, :], V4[:, :, 0], V4[:, :, 1],
                                             ALU.max), reads=[g_s], writes=[g_gs])
    for j in (2, 3):
        p.op("vector", lambda e, j=j: e.tensor_tensor(g_gs[:, :], g_gs[:, :],
                                                      V4[:, :, j], ALU.max),
             reads=[g_s, g_gs], writes=[g_gs])
    G4 = g_gs[:, :].rearrange("p (t g) -> p t g", g=4)
    p.op("vector", lambda e: e.tensor_tensor(g_r1[:, :], G4[:, :, 0], G4[:, :, 1],
                                             ALU.max), reads=[g_gs], writes=[g_r1])
    p.op("vector", lambda e: e.tensor_tensor(g_r2[:, :], G4[:, :, 0], G4[:, :, 1],
                                             ALU.min), reads=[g_gs], writes=[g_r2])
    p.op("vector", lambda e: e.tensor_tensor(g_r3[:, :], G4[:, :, 2], G4[:, :, 3],
                                             ALU.max), reads=[g_gs], writes=[g_r3])
    p.op("vector", lambda e: e.tensor_tensor(g_r4[:, :], G4[:, :, 2], G4[:, :, 3],
                                             ALU.min), reads=[g_gs], writes=[g_r4])
    p.op("vector", lambda e: e.tensor_tensor(g_r1[:, :], g_r1[:, :], g_r3[:, :],
                                             ALU.min), reads=[g_r1, g_r3], writes=[g_r1])
    p.op("vector", lambda e: e.tensor_tensor(g_r2[:, :], g_r2[:, :], g_r4[:, :],
                                             ALU.max), reads=[g_r2, g_r4], writes=[g_r2])
    p.op("vector", lambda e: e.tensor_tensor(g_r1[:, :], g_r1[:, :], g_r2[:, :],
                                             ALU.max), reads=[g_r1, g_r2], writes=[g_r1])
    p.op("vector", lambda e: e.tensor_tensor(
        g_pen[:, :].rearrange("p (t g) -> p t g", g=4), G4,
        g_r1[:, :].unsqueeze(2).broadcast_to((128, NTT, 4)), ALU.is_ge),
        reads=[g_gs, g_r1], writes=[g_pen])
    p.op("vector", lambda e: e.tensor_scalar(g_pen[:, :], g_pen[:, :], -1.0,
                                             -NEG, ALU.add, ALU.mult),
         reads=[g_pen], writes=[g_pen])
    p.op("vector", lambda e: e.tensor_tensor(
        g_ms[:, :].rearrange("p (t g j) -> p t g j", g=4, j=4),
        g_s[:, :].rearrange("p (t g j) -> p t g j", g=4, j=4),
        g_pen[:, :].rearrange("p (t g) -> p t g", g=4).unsqueeze(3)
        .broadcast_to((128, NTT, 4, 4)), ALU.add),
        reads=[g_s, g_pen], writes=[g_ms])
    p.op("vector", lambda e: e.tensor_copy(g_ms0[:, :], g_ms[:, :]),
         reads=[g_ms], writes=[g_ms0])
    MS3 = g_ms[:, :].rearrange("p (t e) -> p t e", e=E)
    for _r in range(3):
        p.op("vector", lambda e: e.reduce_max(g_r2[:, :], MS3, AX.X),
             reads=[g_ms], writes=[g_r2])
        p.op("vector", lambda e: e.tensor_tensor(
            g_wu[:, :].rearrange("p (t e) -> p t e", e=E), MS3,
            g_r2[:, :].unsqueeze(2).broadcast_to((128, NTT, E)), ALU.is_ge),
            reads=[g_ms, g_r2], writes=[g_wu])
        p.op("vector", lambda e: e.tensor_scalar(g_wu[:, :], g_wu[:, :], -NEG,
                                                 None, ALU.mult),
             reads=[g_wu], writes=[g_wu])
        p.op("vector", lambda e: e.tensor_tensor(g_ms[:, :], g_ms[:, :],
                                                 g_wu[:, :], ALU.subtract),
             reads=[g_ms, g_wu], writes=[g_ms])
    p.op("vector", lambda e: e.reduce_max(g_r2[:, :], MS3, AX.X),
         reads=[g_ms], writes=[g_r2])
    p.op("vector", lambda e: e.tensor_tensor(
        g_wu[:, :].rearrange("p (t e) -> p t e", e=E),
        g_ms0[:, :].rearrange("p (t e) -> p t e", e=E),
        g_r2[:, :].unsqueeze(2).broadcast_to((128, NTT, E)), ALU.is_ge),
        reads=[g_ms0, g_r2], writes=[g_wu])
    p.op("vector", lambda e: e.tensor_tensor(g_wu[:, :], g_wu[:, :], g_s[:, :],
                                             ALU.mult),
         reads=[g_wu, g_s], writes=[g_wu])
    p.op("vector", lambda e: e.reduce_sum(
        g_r3[:, :], g_wu[:, :].rearrange("p (t e) -> p t e", e=E), AX.X),
        reads=[g_wu], writes=[g_r3])
    p.op("vector", lambda e: e.reciprocal(g_r4[:, :], g_r3[:, :]),
         reads=[g_r3], writes=[g_r4])
    p.op("vector", lambda e: e.tensor_scalar(g_r4[:, :], g_r4[:, :], ROUTE_SCALE,
                                             None, ALU.mult),
         reads=[g_r4], writes=[g_r4])
    p.op("vector", lambda e: e.tensor_tensor(
        g_we[:, :].rearrange("p (t e) -> p t e", e=E),
        g_wu[:, :].rearrange("p (t e) -> p t e", e=E),
        g_r4[:, :].unsqueeze(2).broadcast_to((128, NTT, E)), ALU.mult),
        reads=[g_wu, g_r4], writes=[g_we])
    for tt in range(NTT):
        p.op("tensor", lambda e, tt=tt: e.transpose(
            pb[2][0:E, 0:128], g_we[:, tt * E:(tt + 1) * E], IDE[:, :]),
            reads=[g_we, IDE], writes=[pb[2]])
        p.op("scalar", lambda e, tt=tt: e.copy(
            WE[:, tt * 128:(tt + 1) * 128], pb[2][0:E, 0:128]),
            reads=[pb[2]], writes=[WE])

    if "sall" in tap_outs:
        _tapdump(p, nc, tap_outs["sall"], g_s)
    if "we" in tap_outs:
        _tapdump(p, nc, tap_outs["we"], g_we)
    if "x2" in tap_outs:
        _tapdump(p, nc, tap_outs["x2"], x2)

    def ysl(dm, tb):
        t_ = (y_lo if dm < 4 else y_hi)
        off = (dm % 4) * (BPC * T) + tb * TBS
        return t_, t_[:, off:off + TBS]

    for tb in range(NTB):
        for dm in range(DC):
            bank = pb[(tb + dm) % 2]
            p.op("tensor", lambda e, bank=bank, dm=dm, tb=tb: e.matmul(
                bank[:, :], B2B[:, dm * 128:(dm + 1) * 128],
                WE[:, tb * TBS:(tb + 1) * TBS], start=True, stop=True),
                reads=[B2B, WE], writes=[bank])
            yt, ya = ysl(dm, tb)
            p.op("scalar", lambda e, bank=bank, ya=ya: e.copy(ya, bank[:, :]),
                 reads=[bank], writes=[yt])

    p.fence()

    p.dma("sync", "cst", OHS[:, :], d_oh.ap())
    p.dma("sync", "cst", B1T[:, :], d_b1t.ap())
    p.dma("sync", "cst", B3T[:, :], d_b3t.ap())

    hp = [HP0, HP0]
    for ex in range(E):
        p.dma("gpsimd", "wgt", W2s[:, :], d_hw2.ap()[ex])
        for tb in range(NTB):
            wrs = [WRS0, WRS1][tb % 2]
            wbank = pb[6 + tb % 2]
            p.op("tensor", lambda e, wbank=wbank, ex=ex, tb=tb: e.matmul(
                wbank[:, :], OHS[:, ex * 128:(ex + 1) * 128],
                WE[:, tb * TBS:(tb + 1) * TBS], start=True, stop=True),
                reads=[OHS, WE], writes=[wbank])
            p.op("scalar", lambda e, wbank=wbank, wrs=wrs: e.copy(wrs[:, :],
                                                                  wbank[:, :]),
                 reads=[wbank], writes=[wrs])
            hpt = hp[tb % 2]
            for im in range(DC):
                abank = pb[(im % 2) * 2]
                bbank = pb[1 + (im % 2) * 2]
                for dc in range(DC):
                    xsl = x2[:, dc * (BPC * T) + tb * TBS:
                             dc * (BPC * T) + (tb + 1) * TBS]
                    p.op("tensor", lambda e, abank=abank, im=im, dc=dc, xsl=xsl:
                         e.matmul(abank[:, :],
                                  W1s[:, im * 1024 + dc * 128: im * 1024 + (dc + 1) * 128],
                                  xsl, start=(dc == 0), stop=(dc == DC - 1)),
                         reads=[W1s, x2], writes=[abank])
                for dc in range(DC):
                    xsl = x2[:, dc * (BPC * T) + tb * TBS:
                             dc * (BPC * T) + (tb + 1) * TBS]
                    p.op("tensor", lambda e, bbank=bbank, im=im, dc=dc, xsl=xsl:
                         e.matmul(bbank[:, :],
                                  W3s[:, im * 1024 + dc * 128: im * 1024 + (dc + 1) * 128],
                                  xsl, start=(dc == 0), stop=(dc == DC - 1)),
                         reads=[W3s, x2], writes=[bbank])
                sa = [XSQ0, XSQ1][im % 2]
                p.op("scalar", lambda e, abank=abank, sa=sa, im=im, ex=ex:
                     e.activation(sa[:, :], abank[:, :], AF.Silu,
                                  bias=B1T[:, im * E + ex: im * E + ex + 1]),
                     reads=[abank, B1T], writes=[sa])
                p.op("vector", lambda e, bbank=bbank, sa=sa, im=im, ex=ex:
                     e.scalar_tensor_tensor(TN[:, :], bbank[:, :],
                                            B3T[:, im * E + ex: im * E + ex + 1],
                                            sa[:, :], ALU.add, ALU.mult),
                     reads=[bbank, B3T, sa], writes=[TN])
                p.op("vector", lambda e, hpt=hpt, im=im, wrs=wrs: e.tensor_tensor(
                    hpt[:, im * TBS:(im + 1) * TBS], TN[:, :], wrs[:, :], ALU.mult),
                    reads=[TN, wrs], writes=[hpt])
            if tb == NTB - 1 and ex + 1 < E:
                p.dma("gpsimd", "wgt", W1s[:, :], d_hw1.ap()[ex + 1])
                p.dma("gpsimd", "wgt", W3s[:, :], d_hw3.ap()[ex + 1])
            _w2pass(p, pb, W2s, hp[tb % 2], ysl, tb)

    if "yacc" in tap_outs:
        _tapdump2(p, nc, tap_outs["yacc"], y_lo, y_hi)

    p.fence()

    for tb in range(NTB):
        xo = [XO0, XO1][tb % 2]
        for cc in range(DC):
            yt, ya = ysl(cc, tb)
            p.op("vector", lambda e, cc=cc, tb=tb, ya=ya: e.tensor_tensor(
                XR[:, cc * TBS:(cc + 1) * TBS],
                x2[:, cc * (BPC * T) + tb * TBS: cc * (BPC * T) + (tb + 1) * TBS],
                ya, ALU.add),
                reads=[x2, yt], writes=[XR])
        _ln(p, nc, pb, XR, XSQ0, XSQ1, TN, MROW, SROW, ALPHA, BETA,
            ONESC, ONES1, EPSC, dst=None, dst_off=0, dst_f32=xo)
        bq = tb // NTB_B
        tloc = (tb % NTB_B) * TBS
        p.dma("sync", "out",
              d_out.ap()[bq].rearrange("(c p) t -> p c t", p=128)[:, :, tloc:tloc + TBS],
              xo[:, :].rearrange("p (c t) -> p c t", c=DC))

    p.wait_all("sync")
    p.wait_all("gpsimd")
    p.emit(nc)
    return nc


def _w2pass(p, pb, W2s, hpt, ysl, tb):
    for dm in range(DC):
        ybank = pb[4 + dm % 2]
        for ic in range(DC):
            p.op("tensor", lambda e, ybank=ybank, dm=dm, ic=ic, hpt=hpt: e.matmul(
                ybank[:, :],
                W2s[:, dm * 1024 + ic * 128: dm * 1024 + (ic + 1) * 128],
                hpt[:, ic * TBS:(ic + 1) * TBS],
                start=(ic == 0), stop=(ic == DC - 1)),
                reads=[W2s, hpt], writes=[ybank])
        yt, ya = ysl(dm, tb)
        p.op("vector", lambda e, ybank=ybank, ya=ya: e.tensor_tensor(
            ya, ya, ybank[:, :], ALU.add),
            reads=[yt, ybank], writes=[yt])


def _ln(p, nc, pb, XR, XSQ0, XSQ1, TN, MROW, SROW, ALPHA, BETA, ONESC, ONES1,
        EPSC, dst, dst_off, dst_f32):
    DCL = DC
    sbank, qbank, abank_, bbank_ = pb[2], pb[3], pb[4], pb[7]
    for cc in range(DCL):
        xsq = [XSQ0, XSQ1][cc % 2]
        p.op("scalar", lambda e, cc=cc, xsq=xsq: e.activation(
            xsq[:, :], XR[:, cc * TBS:(cc + 1) * TBS], AF.Square),
            reads=[XR], writes=[xsq])
        p.op("tensor", lambda e, cc=cc: e.matmul(
            sbank[0:1, :], ONESC[:, 0:1], XR[:, cc * TBS:(cc + 1) * TBS],
            start=(cc == 0), stop=(cc == DCL - 1)),
            reads=[ONESC, XR], writes=[sbank])
        p.op("tensor", lambda e, cc=cc, xsq=xsq: e.matmul(
            qbank[0:1, :], ONESC[:, 0:1], xsq[:, :],
            start=(cc == 0), stop=(cc == DCL - 1)),
            reads=[ONESC, xsq], writes=[qbank])
    inv = 1.0 / float(C)
    p.op("vector", lambda e: e.tensor_scalar(MROW[:, :], sbank[0:1, :], inv, None,
                                             ALU.mult),
         reads=[sbank], writes=[MROW])
    p.op("vector", lambda e: e.tensor_scalar(SROW[:, :], qbank[0:1, :], inv, None,
                                             ALU.mult),
         reads=[qbank], writes=[SROW])
    p.op("vector", lambda e: e.tensor_tensor(ALPHA[:, :], MROW[:, :], MROW[:, :],
                                             ALU.mult),
         reads=[MROW], writes=[ALPHA])
    p.op("vector", lambda e: e.tensor_tensor(SROW[:, :], SROW[:, :], ALPHA[:, :],
                                             ALU.subtract),
         reads=[SROW, ALPHA], writes=[SROW])
    p.op("scalar", lambda e: e.activation(BETA[:, :], SROW[:, :], AF.Ln,
                                          bias=EPSC[0:1, 0:1]),
         reads=[SROW, EPSC], writes=[BETA])
    p.op("scalar", lambda e: e.activation(ALPHA[:, :], BETA[:, :], AF.Exp,
                                          scale=-0.5),
         reads=[BETA], writes=[ALPHA])
    p.op("vector", lambda e: e.scalar_tensor_tensor(BETA[:, :], MROW[:, :], -1.0,
                                                    ALPHA[:, :], ALU.mult, ALU.mult),
         reads=[MROW, ALPHA], writes=[BETA])
    p.op("tensor", lambda e: e.matmul(abank_[:, :], ONES1[:, :], ALPHA[:, :],
                                      start=True, stop=True),
         reads=[ONES1, ALPHA], writes=[abank_])
    p.op("tensor", lambda e: e.matmul(bbank_[:, :], ONES1[:, :], BETA[:, :],
                                      start=True, stop=True),
         reads=[ONES1, BETA], writes=[bbank_])
    for cc in range(DCL):
        p.op("vector", lambda e, cc=cc: e.tensor_tensor(
            TN[:, :], XR[:, cc * TBS:(cc + 1) * TBS], abank_[:, :], ALU.mult),
            reads=[XR, abank_], writes=[TN])
        if dst is not None:
            dsl = dst[:, cc * (BPC * T) + dst_off: cc * (BPC * T) + dst_off + TBS]
            p.op("vector", lambda e, dsl=dsl: e.tensor_tensor(
                dsl, TN[:, :], bbank_[:, :], ALU.add),
                reads=[TN, bbank_], writes=[dst])
        else:
            p.op("vector", lambda e, cc=cc: e.tensor_tensor(
                dst_f32[:, cc * TBS:(cc + 1) * TBS], TN[:, :], bbank_[:, :],
                ALU.add),
                reads=[TN, bbank_], writes=[dst_f32])


def _tapdump(p, nc, dram, sb):
    p.dma("gpsimd", "tap", dram.ap(), sb[:, :])


def _tapdump2(p, nc, dram, lo, hi):
    n = lo.shape[1]
    p.dma("gpsimd", "tap", dram.ap()[:, 0:n], lo[:, :])
    p.dma("gpsimd", "tap", dram.ap()[:, n:2 * n], hi[:, :])



def _run(inputs, taps=()):
    shared = _prep_shared(inputs)
    x = np.asarray(inputs["L0_attn_x"], np.float32)
    nc = build(taps=taps)
    in_maps = []
    for c in range(NCORES):
        m = dict(shared)
        m["xcm"] = np.ascontiguousarray(
            x[c * BPC:(c + 1) * BPC].transpose(0, 2, 1))
        in_maps.append(m)
    res = run_bass_kernel_spmd(nc, in_maps, core_ids=list(range(NCORES)))
    outs = []
    for c in range(NCORES):
        o = res.results[c]["out"]
        outs.append(o)
    full = np.concatenate(outs, axis=0)
    return np.ascontiguousarray(full.transpose(0, 2, 1)), res


def kernel(**inputs):
    out, _ = _run(inputs, taps=_TAPS)
    return out
